# version 03
# baseline (speedup 1.0000x reference)
import numpy as np
import ml_dtypes
import concourse.bass as bass
import concourse.mybir as mybir
from concourse.bass_utils import run_bass_kernel_spmd
from contextlib import ExitStack

F32 = mybir.dt.float32
BF16 = mybir.dt.bfloat16
AF = mybir.ActivationFunctionType
ALU = mybir.AluOpType
AX = mybir.AxisListType
BF = ml_dtypes.bfloat16

B, T, D = 4, 4096, 1024
H, DK = 16, 64
TOK = 2048
ROPE_BASE = 6000.0
LN_EPS = 1e-5


def cdiv(a, b):
    return -(-a // b)


class Prog:

    def __init__(self):
        self.nc = bass.Bass("TRN2", target_bir_lowering=False, debug=False)
        self.ctx = ExitStack()
        self.sems = {}
        self.cnt = {}

    def sem(self, name):
        if name not in self.sems:
            self.sems[name] = self.ctx.enter_context(self.nc.semaphore(name))
            self.cnt[name] = 0
        return self.sems[name]

    def inc(self, instr, name, n=1):
        instr.then_inc(self.sem(name), n)
        self.cnt[name] += n
        return self.cnt[name]

    def val(self, name):
        self.sem(name)
        return self.cnt[name]

    def incd(self, eng, name, n=1):
        ins = eng.drain()
        ins.then_inc(self.sem(name), n)
        self.cnt[name] += n
        return self.cnt[name]

    def wait(self, eng, name, v=None):
        v = self.cnt[name] if v is None else v
        if v > 0:
            eng.wait_ge(self.sem(name), v)

    def sbuf(self, name, shape, dtype):
        return self.ctx.enter_context(self.nc.sbuf_tensor(name, shape, dtype))

    def psum(self, name, shape, dtype):
        return self.ctx.enter_context(self.nc.psum_tensor(name, shape, dtype))

    def const(self, name, arr):
        if not hasattr(self, "_consts"):
            self._consts = {}
        if name not in self._consts:
            self._consts[name] = self.nc.inline_tensor(
                np.ascontiguousarray(arr), name=name).ap()
        return self._consts[name]

    def fence_loads(self):
        for q in (self.nc.sync, self.nc.gpsimd):
            for sem in ("pe", "ve", "se"):
                self.wait(q, sem, self.val(sem))

    def sbuf_scoped(self, name, shape, dtype):
        cm = self.nc.sbuf_tensor(getattr(self, "prefix", "") + name, shape, dtype)
        h = cm.__enter__()
        return h, (lambda: cm.__exit__(None, None, None))

    def finish(self):
        self.ctx.close()
        return self.nc


class Scope:

    def __init__(self, p, prefix=""):
        self.p = p
        self.prefix = prefix
        p.prefix = prefix
        self.cls = []

    def sbuf(self, name, shape, dtype):
        h, cl = self.p.sbuf_scoped(name, shape, dtype)
        self.cls.append(cl)
        return h

    def psum(self, name, shape, dtype):
        cm = self.p.nc.psum_tensor(self.prefix + name, shape, dtype)
        h = cm.__enter__()
        self.cls.append(lambda cm=cm: cm.__exit__(None, None, None))
        return h

    def close(self):
        for cl in reversed(self.cls):
            cl()
        self.p.prefix = ""
        self.p.fence_loads()


def _consts():
    c = {}
    c["ident"] = np.eye(128, dtype=BF)
    A = np.zeros((64, 64), np.float32)
    for i in range(16):
        A[i + 16, i] = -1.0
        A[i, i + 16] = 1.0
    A2 = np.zeros((128, 128), np.float32)
    A2[0:64, 0:64] = A
    A2[64:128, 64:128] = A
    c["ropeA"] = A2.astype(BF)
    inv_freq = 1.0 / ROPE_BASE ** (np.arange(0, 32, 2, dtype=np.float32) / 32.0)
    t = np.arange(T, dtype=np.float32)
    fr = np.outer(inv_freq, t)
    cos64 = np.ones((64, T), np.float32)
    sin64 = np.zeros((64, T), np.float32)
    cos64[0:16] = np.cos(fr); cos64[16:32] = np.cos(fr)
    sin64[0:16] = np.sin(fr); sin64[16:32] = np.sin(fr)
    c["cosA"] = np.concatenate([cos64, cos64], 0).astype(BF)
    c["sinA"] = np.concatenate([sin64, sin64], 0).astype(BF)
    kr = np.arange(128)[:, None]
    qc = np.arange(128)[None, :]
    c["triT"] = (kr <= qc).astype(BF)
    dd = kr - qc
    c["bandL"] = ((dd >= 112) & (dd <= 127)).astype(BF)
    c["bandR"] = ((dd >= -16) & (dd <= -1)).astype(BF)
    c["ones_row"] = np.ones((1, 128), np.float32)
    return c


def emit_ln(p, x_src, ntiles, xn_fm, bufs):
    nc = p.nc
    v, s, te, sy = nc.vector, nc.scalar, nc.tensor, nc.sync
    xt2, scr, xn_tm, stats = bufs["x_tile"], bufs["scr"], bufs["xn_tm"], bufs["stats"]
    ps_tr, ident = bufs["ps_tr"], bufs["ident"]
    free_v = [0, 0]
    free_s = [0, 0]
    tr_free = 0
    for ti in range(ntiles):
        par = ti % 2
        xt = xt2[par]
        p.wait(sy, "ve", free_v[par])
        p.wait(sy, "se", free_s[par])
        ldv = p.inc(sy.dma_start(out=xt[:], in_=x_src[128 * ti:128 * (ti + 1), :]), "w", 16)
        p.wait(s, "w", ldv)
        s.square(scr[:], xt[:])
        sqv = p.incd(s, "se")
        free_s[par] = sqv
        p.wait(v, "w", ldv)
        v.tensor_reduce(stats[:, 0:1], xt[:], axis=AX.X, op=ALU.add)
        p.wait(v, "se", sqv)
        v.tensor_reduce(stats[:, 1:2], scr[:], axis=AX.X, op=ALU.add)
        v.drain()
        v.tensor_scalar(stats[:, 2:3], stats[:, 0:1], 1.0 / 1024.0, None, op0=ALU.mult)
        v.tensor_scalar(stats[:, 3:4], stats[:, 1:2], 1.0 / 1024.0, None, op0=ALU.mult)
        v.drain()
        v.tensor_tensor(stats[:, 4:5], stats[:, 2:3], stats[:, 2:3], op=ALU.mult)
        v.drain()
        v.tensor_tensor(stats[:, 5:6], stats[:, 3:4], stats[:, 4:5], op=ALU.subtract)
        v.drain()
        v.tensor_scalar(stats[:, 5:6], stats[:, 5:6], LN_EPS, None, op0=ALU.add)
        vv = p.incd(v, "ve")
        p.wait(s, "ve", vv)
        s.activation(stats[:, 7:8], stats[:, 5:6], AF.Sqrt)
        sv = p.incd(s, "se")
        p.wait(v, "se", sv)
        v.reciprocal(stats[:, 6:7], stats[:, 7:8])
        v.drain()
        v.tensor_scalar(xn_tm[:], xt[:], stats[:, 2:3], stats[:, 6:7],
                        op0=ALU.subtract, op1=ALU.mult)
        xnv = p.incd(v, "ve")
        free_v[par] = xnv
        p.wait(te, "ve", xnv)
        p.wait(te, "ve", tr_free)
        for ch in range(8):
            mm = te.transpose(ps_tr[:, 128 * ch:128 * (ch + 1)],
                              xn_tm[:, 128 * ch:128 * (ch + 1)], ident)
        pev = p.inc(mm, "pe")
        p.wait(v, "pe", pev)
        v.tensor_copy(xn_fm[:, :, 128 * ti:128 * (ti + 1)],
                      ps_tr.ap().rearrange("p (c f) -> p c f", c=8))
        tr_free = p.incd(v, "ve")
    return tr_free


def build_launchA(debug=False):
    p = Prog()
    nc = p.nc
    xA = nc.dram_tensor("xA", [T, D], F32, kind="ExternalInput").ap()
    wqkvA = nc.dram_tensor("wqkvA", [3, D, 512], F32, kind="ExternalInput").ap()
    preWoA = nc.dram_tensor("preWoA", [128, 4, T], BF16, kind="ExternalOutput").ap()
    sc = Scope(p, "A_")
    emit_A(p, sc, xA, wqkvA, preWoA)
    sc.close()
    return p.finish()


def emit_A(p, sc, xA, wqkvA, preWoA):
    nc = p.nc
    v, s, te, sy, g = nc.vector, nc.scalar, nc.tensor, nc.sync, nc.gpsimd
    C = _consts()

    ident = sc.sbuf("identS", [128, 128], BF16)
    ropeA = sc.sbuf("ropeS", [128, 128], BF16)
    cosA = sc.sbuf("cosS", [128, T], BF16)
    sinA = sc.sbuf("sinS", [128, T], BF16)
    triT = sc.sbuf("triS", [128, 128], BF16)

    x_tile = [sc.sbuf("x_tile0", [128, 1024], F32), sc.sbuf("x_tile1", [128, 1024], F32)]
    scr = sc.sbuf("scr", [128, 1024], F32)
    xn_tm = sc.sbuf("xn_tm", [128, 1024], BF16)
    stats = sc.sbuf("stats", [128, 8], F32)
    xn_fm = sc.sbuf("xn_fm", [128, 8, T], BF16)
    w_sb = [[sc.sbuf(f"w_sb{i}{j}", [128, 8, 128], BF16) for j in range(3)]
            for i in range(2)]
    evac = sc.sbuf("evac", [128, 512], BF16)
    evB = sc.sbuf("evB", [64, 512], BF16)
    tc_sb = sc.sbuf("tc_sb", [128, 512], BF16)
    ts_sb = sc.sbuf("ts_sb", [128, 512], BF16)
    k_rot = sc.sbuf("k_rot", [128, T], BF16)
    q_rot = sc.sbuf("q_rot", [128, T], BF16)
    vtok = sc.sbuf("vtok", [128, 2, 32, 65], BF16)
    s_row = sc.sbuf("s_row", [128, T], BF16)
    p_sb = sc.sbuf("p_sb", [128, 512], BF16)
    pt_sb = sc.sbuf("pt_sb", [128, 512], BF16)
    out64 = sc.sbuf("out64", [128, 64], BF16)
    stats2 = sc.sbuf("stats2", [128, 4], F32)
    preWo = sc.sbuf("preWo", [128, 4, T], BF16)

    ps_proj = sc.psum("ps_proj", [128, 512], F32)
    ps_swp = sc.psum("ps_swp", [128, 512], F32)
    ps_tr = sc.psum("ps_tr", [128, 1024], BF16)
    ps_s = [sc.psum("ps_s0", [128, 512], F32), sc.psum("ps_s1", [128, 512], F32)]
    ps_acc = sc.psum("ps_acc", [128, 512], F32)

    for nm, sb in (("ident", ident), ("ropeA", ropeA), ("cosA", cosA),
                   ("sinA", sinA), ("triT", triT)):
        p.inc(sy.dma_start(out=sb[:], in_=p.const(nm + "_c", C[nm])), "w", 16)
    wld = [[0] * 3 for _ in range(2)]
    wfree = [0, 0]
    for pair in range(2):
        for j in range(3):
            wld[pair][j] = p.inc(g.dma_start(
                out=w_sb[pair][j][:],
                in_=wqkvA[j, :, 128 * pair:128 * (pair + 1)]
                .rearrange("(c p) m -> p c m", p=128)), "w", 16)
    v.memset(vtok[:, :, :, 64:65], 1.0)

    bufs = dict(x_tile=x_tile, scr=scr, xn_tm=xn_tm, stats=stats, ps_tr=ps_tr,
                ident=ident[:])
    emit_ln(p, xA, T // 128, xn_fm, bufs)

    NB = T // 512
    rel = {"ps_proj": ("se", 0), "ps_swp": ("ve", 0), "ps_tr": ("ve", 0),
           "evac": ("ve", 0), "ps_acc": ("ve", 0)}

    def wait_rel(eng, key):
        sem, c = rel[key]
        p.wait(eng, sem, c)

    for pair in range(4):
        par = pair % 2
        if pair >= 2:
            for j in range(3):
                p.wait(g, "pe", wfree[par])
                wld[par][j] = p.inc(g.dma_start(
                    out=w_sb[par][j][:],
                    in_=wqkvA[j, :, 128 * pair:128 * (pair + 1)]
                    .rearrange("(c p) m -> p c m", p=128)), "w", 16)
        wq, wk, wv = w_sb[par]
        pair_pe_gate = p.val("pe")

        for which, dst in (("k", k_rot), ("q", q_rot)):
            wmat = wk if which == "k" else wq
            wl = wld[par][1 if which == "k" else 0]
            for tb in range(NB):
                sl = slice(512 * tb, 512 * (tb + 1))
                p.wait(te, "w", wl)
                wait_rel(te, "ps_proj")
                for kc in range(8):
                    mm = te.matmul(ps_proj[:], wmat[:, kc, :], xn_fm[:, kc, sl],
                                   start=(kc == 0), stop=(kc == 7))
                pev = p.inc(mm, "pe")
                p.wait(s, "pe", pev)
                wait_rel(s, "evac")
                s.copy(evac[:], ps_proj[:])
                sev = p.incd(s, "se")
                rel["ps_proj"] = ("se", sev)
                p.wait(te, "se", sev)
                wait_rel(te, "ps_swp")
                mm = te.matmul(ps_swp[:], ropeA[:], evac[:], start=True, stop=True)
                pev2 = p.inc(mm, "pe")
                p.wait(v, "se", sev)
                p.wait(v, "pe", max(pev2, pair_pe_gate))
                v.tensor_tensor(tc_sb[:], evac[:], cosA[:, sl], op=ALU.mult)
                v.tensor_tensor(ts_sb[:], ps_swp[:], sinA[:, sl], op=ALU.mult)
                v.tensor_tensor(dst[:, sl], tc_sb[:], ts_sb[:], op=ALU.add)
                vev = p.incd(v, "ve")
                rel["ps_swp"] = ("ve", vev)
                rel["evac"] = ("ve", vev)

        for tb in range(NB):
            sl = slice(512 * tb, 512 * (tb + 1))
            p.wait(te, "w", wld[par][2])
            wait_rel(te, "ps_proj")
            for kc in range(8):
                mm = te.matmul(ps_proj[:], wv[:, kc, :], xn_fm[:, kc, sl],
                               start=(kc == 0), stop=(kc == 7))
            pev = p.inc(mm, "pe")
            p.wait(s, "pe", pev)
            wait_rel(s, "evac")
            s.copy(evac[:], ps_proj[:])
            sev = p.incd(s, "se")
            rel["ps_proj"] = ("se", sev)
            p.wait(v, "se", sev)
            v.tensor_copy(evB[:], evac[64:128, :])
            bev = p.incd(v, "ve")
            p.wait(te, "ve", bev)
            wait_rel(te, "ps_tr")
            for h2 in range(2):
                for j4 in range(4):
                    src = evac[0:64, 128 * j4:128 * (j4 + 1)] if h2 == 0 else \
                          evB[:, 128 * j4:128 * (j4 + 1)]
                    mm = te.transpose(ps_tr[:, h2 * 256 + j4 * 64:h2 * 256 + (j4 + 1) * 64],
                                      src, ident[0:64, 0:64])
            pev = p.inc(mm, "pe")
            p.wait(v, "pe", max(pev, pair_pe_gate))
            src3 = ps_tr[:, 0:512].rearrange("p (two j f) -> p two j f", two=2, j=4)
            for h2 in range(2):
                v.tensor_copy(vtok[:, h2, 4 * tb:4 * tb + 4, 0:64],
                              src3[:, h2, :, :])
            vev = p.incd(v, "ve")
            rel["ps_tr"] = ("ve", vev)
            rel["evac"] = ("ve", vev)
        wfree[par] = p.val("pe")

        ldone = [0, 0]
        for h2 in range(2):
            hb = 64 * h2
            for i in range(32):
                prefix = 128 * (i + 1)
                nblk = cdiv(prefix, 512)
                for j in range(nblk):
                    w_ = min(512, prefix - 512 * j)
                    bank = ps_s[j % 2]
                    p.wait(te, "se", ldone[j % 2])
                    mm = te.matmul(bank[:, 0:w_],
                                   q_rot[hb:hb + 64, 128 * i:128 * (i + 1)],
                                   k_rot[hb:hb + 64, 512 * j:512 * j + w_],
                                   start=True, stop=True)
                    pev = p.inc(mm, "pe")
                    p.wait(s, "pe", pev)
                    s.copy(s_row[:, 512 * j:512 * j + w_], bank[:, 0:w_])
                    ldone[j % 2] = p.incd(s, "se")
                p.wait(v, "se", p.val("se"))
                v.tensor_reduce(stats2[:, 0:1], s_row[:, 0:prefix], axis=AX.X,
                                op=ALU.max)
                v.drain()
                v.tensor_scalar(stats2[:, 1:2], stats2[:, 0:1], -0.125, None,
                                op0=ALU.mult)
                pov = p.incd(v, "ve")
                p.wait(s, "ve", pov)
                first = True
                pt_done = 0
                for j in range(nblk):
                    w_ = min(512, prefix - 512 * j)
                    ncc = w_ // 128
                    p.wait(s, "pe", pt_done)
                    s.activation(p_sb[:, 0:w_], s_row[:, 512 * j:512 * j + w_],
                                 AF.Exp, bias=stats2[:, 1:2], scale=0.125)
                    sev = p.incd(s, "se")
                    p.wait(te, "se", sev)
                    wait_rel(te, "ps_tr")
                    for cc in range(ncc):
                        mm = te.transpose(ps_tr[:, 128 * cc:128 * (cc + 1)],
                                          p_sb[:, 128 * cc:128 * (cc + 1)], ident[:])
                    pev = p.inc(mm, "pe")
                    pt_done = pev
                    p.wait(v, "pe", pev)
                    diag = (j == nblk - 1)
                    if diag and ncc > 1:
                        v.tensor_copy(pt_sb[:, 0:128 * (ncc - 1)],
                                      ps_tr[:, 0:128 * (ncc - 1)])
                    if diag:
                        ev = v.tensor_tensor(pt_sb[:, 128 * (ncc - 1):128 * ncc],
                                             ps_tr[:, 128 * (ncc - 1):128 * ncc],
                                             triT[:], op=ALU.mult)
                    else:
                        ev = v.tensor_copy(pt_sb[:, 0:w_], ps_tr[:, 0:w_])
                    vev = p.incd(v, "ve")
                    rel["ps_tr"] = ("ve", vev)
                    p.wait(te, "ve", vev)
                    if first:
                        wait_rel(te, "ps_acc")
                    for cc in range(ncc):
                        kt = 4 * j + cc
                        mm = te.matmul(ps_acc[:, 0:65],
                                       pt_sb[:, 128 * cc:128 * (cc + 1)],
                                       vtok[:, h2, kt, :], start=first,
                                       stop=(j == nblk - 1 and cc == ncc - 1))
                        first = False
                pev = p.inc(mm, "pe")
                p.wait(v, "pe", pev)
                v.reciprocal(stats2[:, 2:3], ps_acc[:, 64:65])
                v.drain()
                v.tensor_scalar(out64[:], ps_acc[:, 0:64], stats2[:, 2:3], None,
                                op0=ALU.mult)
                accv = p.incd(v, "ve")
                rel["ps_acc"] = ("ve", accv)
                p.wait(te, "ve", accv)
                wait_rel(te, "ps_tr")
                mm = te.transpose(ps_tr[0:64, 896:1024], out64[:], ident[:])
                pev = p.inc(mm, "pe")
                p.wait(v, "pe", pev)
                v.tensor_copy(preWo[hb:hb + 64, pair, 128 * i:128 * (i + 1)],
                              ps_tr[0:64, 896:1024])
                vev = p.incd(v, "ve")
                rel["ps_tr"] = ("ve", vev)

    p.wait(sy, "ve", p.val("ve"))
    p.inc(sy.dma_start(out=preWoA, in_=preWo[:]), "st", 16)
    p.wait(sy, "st", p.val("st"))


_progA = None


def run_launchA(x, wqkv_all):
    global _progA
    if _progA is None:
        _progA = build_launchA()
    in_maps = []
    for c in range(8):
        b, hg = c // 2, c % 2
        in_maps.append({
            "xA": np.ascontiguousarray(x[b]),
            "wqkvA": np.ascontiguousarray(wqkv_all[:, :, 512 * hg:512 * (hg + 1)]),
        })
    res = run_bass_kernel_spmd(_progA, in_maps, core_ids=list(range(8)))
    return [r["preWoA"] for r in res.results]


def prep_qkv(inp):
    return np.ascontiguousarray(
        np.transpose(inp["L0_attn_W_qkv"], (0, 3, 1, 2)).reshape(3, D, H * DK))





def declare_B_weights(nc):
    d = {}
    d["WoT_d"] = nc.dram_tensor("WoT", [D, D], F32, kind="ExternalInput").ap()
    d["gwT_d"] = nc.dram_tensor("gwT", [D, 8], F32, kind="ExternalInput").ap()
    d["gb_d"] = nc.dram_tensor("gb", [1, 8], F32, kind="ExternalInput").ap()
    wstr = {}
    for nm in ("w1g", "w1l", "w3g", "w3l"):
        wstr[nm] = nc.dram_tensor(nm, [8, D, 2048], F32, kind="ExternalInput").ap()
        wstr["s" + nm] = nc.dram_tensor("s" + nm, [D, 2048], F32, kind="ExternalInput").ap()
    d["wstr"] = wstr
    d["w2_d"] = nc.dram_tensor("w2", [8, 2048, D], F32, kind="ExternalInput").ap()
    d["sw2_d"] = nc.dram_tensor("sw2", [2048, D], F32, kind="ExternalInput").ap()
    bstr = {}
    for nm in ("b1g", "b1l", "b3g", "b3l"):
        bstr[nm] = nc.dram_tensor(nm, [8, 128, 16], F32, kind="ExternalInput").ap()
        bstr["s" + nm] = nc.dram_tensor("s" + nm, [128, 16], F32, kind="ExternalInput").ap()
    d["bstr"] = bstr
    d["b2_d"] = nc.dram_tensor("b2", [8, D], F32, kind="ExternalInput").ap()
    d["sb2_d"] = nc.dram_tensor("sb2", [1, D], F32, kind="ExternalInput").ap()
    return d


def build_launchB(debug=False):
    p = Prog()
    nc = p.nc
    xB = nc.dram_tensor("xB", [TOK, D], F32, kind="ExternalInput").ap()
    preWoB = nc.dram_tensor("preWoB", [128, 8, TOK], BF16, kind="ExternalInput").ap()
    dw = declare_B_weights(nc)
    x1B = nc.dram_tensor("x1B", [128, 8, TOK], F32, kind="ExternalOutput").ap()
    s0_d = nc.dram_tensor("s0scr", [128, 8, TOK], BF16, kind="Internal").ap()
    comb_d = nc.dram_tensor("combscr", [8, TOK], F32, kind="Internal").ap()
    sc = Scope(p, "B_")
    emit_B(p, sc, xB, None, preWoB, dw, x1B, s0_d, comb_d)
    sc.close()
    return p.finish()


def emit_B(p, sc, xB, preWo_ld, preWoB, dw, x1B, s0_d, comb_d, debug=False):
    nc = p.nc
    v, s, te, sy, g = nc.vector, nc.scalar, nc.tensor, nc.sync, nc.gpsimd
    C = _consts()
    WoT_d, gwT_d, gb_d = dw["WoT_d"], dw["gwT_d"], dw["gb_d"]
    wstr, w2_d, sw2_d = dw["wstr"], dw["w2_d"], dw["sw2_d"]
    bstr, b2_d, sb2_d = dw["bstr"], dw["b2_d"], dw["sb2_d"]
    ident = sc.sbuf("identS", [128, 128], BF16)
    identF = sc.sbuf("identF", [128, 128], F32)
    onesr = sc.sbuf("onesr", [1, 128], F32)
    p.inc(sy.dma_start(out=ident[:], in_=p.const("ident_c", C["ident"])), "w", 16)
    p.inc(sy.dma_start(out=identF[:], in_=p.const("identF_c", np.eye(128, dtype=np.float32))), "w", 16)
    p.inc(sy.dma_start(out=onesr[:], in_=p.const("onesr_c", C["ones_row"])), "w", 16)

    x_tile = [sc.sbuf("x_tile0", [128, 1024], F32), sc.sbuf("x_tile1", [128, 1024], F32)]
    xn_fm = sc.sbuf("xn_fm", [128, 8, TOK], BF16)
    stats = sc.sbuf("stats", [128, 8], F32)
    ps_up = [sc.psum(f"ps_up{i}", [128, 512], F32) for i in range(4)]
    ps_dn = [sc.psum(f"ps_dn{i}", [128, 512], F32) for i in range(2)]
    ps_tr = sc.psum("ps_tr", [128, 1024], BF16)
    ps_trf = sc.psum("ps_trf", [128, 512], F32)

    scrh, scr_cl = p.sbuf_scoped("scrLN", [128, 1024], F32)
    xn_tmh, xntm_cl = p.sbuf_scoped("xn_tm", [128, 1024], BF16)
    bufs = dict(x_tile=x_tile, scr=scrh, xn_tm=xn_tmh, stats=stats, ps_tr=ps_tr,
                ident=ident[:])
    emit_ln(p, xB, TOK // 128, xn_fm, bufs)
    xntm_cl(); scr_cl()
    p.fence_loads()

    preWo, prew_cl = p.sbuf_scoped("preWo", [128, 8, TOK], BF16)
    woT, woT_cl = p.sbuf_scoped("woT", [128, 8, D], BF16)
    s0t, s0t_cl = p.sbuf_scoped("s0t", [128, 512], BF16)
    if preWo_ld is None:
        p.inc(sy.dma_start(out=preWo[:], in_=preWoB), "w", 16)
    else:
        preWo_ld(preWo)
    p.inc(g.dma_start(out=woT[:], in_=WoT_d.rearrange("(c p) m -> p c m", p=128)), "w", 16)
    p.wait(te, "w", p.val("w"))
    evfree = 0
    for dc in range(8):
        for tbb in range(4):
            sl = slice(512 * tbb, 512 * (tbb + 1))
            p.wait(te, "se", evfree)
            for hc in range(8):
                mm = te.matmul(ps_up[0][:], woT[:, hc, 128 * dc:128 * (dc + 1)],
                               preWo[:, hc, sl], start=(hc == 0), stop=(hc == 7))
            pev = p.inc(mm, "pe")
            p.wait(s, "pe", pev)
            s.copy(s0t[:], ps_up[0][:])
            sev = p.incd(s, "se")
            evfree = sev
            p.wait(sy, "se", sev)
            stv = p.inc(sy.dma_start(out=s0_d[:, dc, sl], in_=s0t[:]), "st", 16)
            p.wait(s, "st", stv)
    s0t_cl(); woT_cl(); prew_cl()
    p.fence_loads()

    gwT, gwT_cl = p.sbuf_scoped("gwT_s", [128, 8, 8], BF16)
    gb_b, gbb_cl = p.sbuf_scoped("gb_b", [128, 8], F32)
    sc_sb, sc_cl = p.sbuf_scoped("sc_sb", [128, 8], F32)
    bi_sb, bi_cl = p.sbuf_scoped("bi_sb", [128, 8], F32)
    cmp8, cmp8_cl = p.sbuf_scoped("cmp8", [128, 8, 8], F32)
    rank, rank_cl = p.sbuf_scoped("rank", [128, 8], F32)
    comb_tm, ctm_cl = p.sbuf_scoped("comb_tm", [128, 16, 8], F32)
    gstat, gstat_cl = p.sbuf_scoped("gstat", [128, 4], F32)
    gbs, gbs_cl = p.sbuf_scoped("gbs", [1, 8], F32)
    p.inc(g.dma_start(out=gwT[:], in_=gwT_d.rearrange("(c p) e -> p c e", p=128)), "w", 16)
    p.inc(sy.dma_start(out=gbs[:], in_=gb_d), "w", 16)
    p.wait(te, "w", p.val("w"))
    mm = te.matmul(ps_trf[0:128, 0:8], onesr[:], gbs[:], start=True, stop=True)
    pev = p.inc(mm, "pe")
    p.wait(v, "pe", pev)
    p.inc(v.tensor_copy(gb_b[:], ps_trf[0:128, 0:8]), "ve")
    rel_ve = p.val("ve")
    rel_se = 0
    for t in range(16):
        p.wait(te, "ve", rel_ve)
        p.wait(te, "se", rel_se)
        for kc in range(8):
            mm = te.matmul(ps_trf[:, 0:8], xn_fm[:, kc, 128 * t:128 * (t + 1)],
                           gwT[:, kc, :], start=(kc == 0), stop=(kc == 7))
        pev = p.inc(mm, "pe")
        p.wait(v, "pe", pev)
        v.tensor_reduce(gstat[:, 0:1], ps_trf[:, 0:8], axis=AX.X, op=ALU.max)
        v.drain()
        v.tensor_scalar(gstat[:, 1:2], gstat[:, 0:1], -1.0, None, op0=ALU.mult)
        vv = p.incd(v, "ve")
        rel_ve = vv
        p.wait(s, "ve", vv)
        s.activation(sc_sb[:], ps_trf[:, 0:8], AF.Exp, bias=gstat[:, 1:2])
        sev = p.incd(s, "se")
        rel_se = sev
        p.wait(v, "se", sev)
        v.tensor_reduce(gstat[:, 2:3], sc_sb[:], axis=AX.X, op=ALU.add)
        v.drain()
        v.reciprocal(gstat[:, 3:4], gstat[:, 2:3])
        v.drain()
        v.tensor_scalar(sc_sb[:], sc_sb[:], gstat[:, 3:4], None, op0=ALU.mult)
        v.drain()
        v.tensor_tensor(bi_sb[:], sc_sb[:], gb_b[:], op=ALU.add)
        v.drain()
        for e in range(8):
            v.tensor_scalar(cmp8[:, e, :], bi_sb[:], bi_sb[:, e:e + 1], None,
                            op0=ALU.is_gt)
        v.drain()
        v.tensor_reduce(rank[:], cmp8.ap(), axis=AX.X, op=ALU.add)
        v.drain()
        v.tensor_scalar(rank[:], rank[:], 2.0, None, op0=ALU.is_lt)
        v.drain()
        v.tensor_tensor(comb_tm[:, t, :], rank[:], sc_sb[:], op=ALU.mult)
        p.incd(v, "ve")
    combT, combT_cl = p.sbuf_scoped("combT", [8, TOK], F32)
    tr_rel = 0
    for t in range(16):
        p.wait(te, "ve", p.val("ve") if t == 0 else tr_rel)
        mm = te.transpose(ps_trf[0:8, 0:128], comb_tm[:, t, :], identF[:])
        pev = p.inc(mm, "pe")
        p.wait(v, "pe", pev)
        v.tensor_copy(combT[:, 128 * t:128 * (t + 1)], ps_trf[0:8, 0:128])
        tr_rel = p.incd(v, "ve")
    p.wait(sy, "ve", tr_rel)
    cdv = p.inc(sy.dma_start(out=comb_d, in_=combT[:]), "st", 16)
    p.wait(sy, "st", cdv)
    combT_cl(); gbs_cl(); gstat_cl(); ctm_cl(); rank_cl(); cmp8_cl(); bi_cl()
    sc_cl(); gbb_cl(); gwT_cl()
    p.fence_loads()

    g0 = sc.sbuf("g0", [128, 8, TOK], F32)
    ffn_scoped = []

    def fsb(name, shape, dtype):
        h, cl = p.sbuf_scoped(name, shape, dtype)
        ffn_scoped.append(cl)
        return h
    comb_bc = fsb("comb_bc", [128, 4, 512], F32)
    wsb = [{nm: fsb(f"{nm}_sb{i}", [128, 8, 256], BF16)
            for nm in ("w1g", "w1l", "w3g", "w3l")} for i in range(2)]
    w2sb = [fsb(f"w2_sb{i}", [128, 2, 1024], BF16) for i in range(2)]
    bsb = [{nm: fsb(f"{nm}_col{i}", [128, 16], F32)
            for nm in ("b1g", "b1l", "b3g", "b3l")} for i in range(2)]
    b2r = [fsb(f"b2r{i}", [1, 1024], F32) for i in range(2)]
    crow = [fsb(f"crow{i}", [1, TOK], F32) for i in range(2)]
    t_w1 = fsb("t_w1", [128, 1024], F32)
    t_w3 = fsb("t_w3", [128, 1024], F32)
    scr_g = fsb("scr_g", [128, 1024], F32)
    scr_s = fsb("scr_s", [128, 1024], F32)
    gs_sb = fsb("gs_sb", [128, 2, 1024], F32)
    h_sb = fsb("h_sb", [128, 2, 1024], BF16)
    crow_ld = [0, 0]
    crow_rel = [(0, 0), (0, 0)]

    def load_crow(e, par):
        pe_r, ve_r = crow_rel[par]
        if e == 8:
            p.wait(v, "pe", pe_r)
            crow_ld[par] = None
            v.memset(crow[par][:], 1.0)
            p.incd(v, "ve")
        else:
            p.wait(sy, "pe", pe_r)
            p.wait(sy, "ve", ve_r)
            crow_ld[par] = p.inc(sy.dma_start(out=crow[par][:],
                                              in_=comb_d[e:e + 1, :]), "w", 16)

    NSLAB = 8
    wbuf_free = [0, 0]
    bbuf_free = [(0, 0), (0, 0)]
    slab_ld = [0, 0]
    bias_ld = [0, 0]

    def load_slab(e, sl8, par):
        shared = (e == 8)
        pre = "s" if shared else ""
        p.wait(g, "pe", wbuf_free[par])
        for nm in ("w1g", "w1l", "w3g", "w3l"):
            src = wstr[pre + nm]
            sli = src[:, 256 * sl8:256 * (sl8 + 1)] if shared else \
                src[e, :, 256 * sl8:256 * (sl8 + 1)]
            ld = p.inc(g.dma_start(out=wsb[par][nm][:],
                                   in_=sli.rearrange("(c p) j -> p c j", p=128)), "w", 16)
        src = sw2_d if shared else w2_d[e]
        ld = p.inc(g.dma_start(out=w2sb[par][:],
                               in_=src[256 * sl8:256 * (sl8 + 1), :]
                               .rearrange("(c p) d -> p c d", p=128)), "w", 16)
        slab_ld[par] = ld

    def load_bias(e, par):
        shared = (e == 8)
        pre = "s" if shared else ""
        pe_r, se_r = bbuf_free[par]
        p.wait(sy, "pe", pe_r)
        p.wait(sy, "se", se_r)
        for nm in ("b1g", "b1l", "b3g", "b3l"):
            src = bstr[pre + nm] if shared else bstr[nm][e]
            p.inc(sy.dma_start(out=bsb[par][nm][:], in_=src), "w", 16)
        src2 = sb2_d if shared else b2_d[e:e + 1, :]
        ld = p.inc(sy.dma_start(out=b2r[par][:], in_=src2), "w", 16)
        bias_ld[par] = ld

    load_slab(0, 0, 0)
    load_bias(0, 0)
    load_crow(0, 0)
    cbc_rel = 0
    dn_rel = [0, 0]
    up_rel = [0, 0, 0, 0]
    h_rel = 0
    tw_rel = 0

    for e in range(9):
        epar = e % 2
        if e < 8:
            load_crow(e + 1, (e + 1) % 2)
        if crow_ld[epar]:
            p.wait(te, "w", crow_ld[epar])
        else:
            p.wait(te, "ve", p.val("ve"))
        for tbb in range(4):
            p.wait(te, "ve", dn_rel[0])
            mm = te.matmul(ps_dn[0][:], onesr[:],
                           crow[epar][0:1, 512 * tbb:512 * (tbb + 1)],
                           start=True, stop=True)
            pev = p.inc(mm, "pe")
            p.wait(v, "pe", pev)
            p.wait(v, "ve", cbc_rel)
            v.tensor_copy(comb_bc[:, tbb, :], ps_dn[0][:])
            dn_rel[0] = p.incd(v, "ve")
        for sl8 in range(NSLAB):
            par = (e * NSLAB + sl8) % 2
            nxt = e * NSLAB + sl8 + 1
            if nxt < 9 * NSLAB:
                load_slab(nxt // NSLAB, nxt % NSLAB, nxt % 2)
                if nxt % NSLAB == 0:
                    load_bias(nxt // NSLAB, (nxt // NSLAB) % 2)
            wl = slab_ld[par]
            bl = bias_ld[epar]
            for tbp in range(2):
                for pas in ("g", "l"):
                    wa, wb = ("w1g", "w3g") if pas == "g" else ("w1l", "w3l")
                    ba, bb = ("b1g", "b3g") if pas == "g" else ("b1l", "b3l")
                    for jc in range(2):
                        p.wait(te, "w", wl)
                        for st_i, wn in ((0, wa), (1, wb)):
                            for tb2 in range(2):
                                bank = ps_up[st_i * 2 + tb2]
                                p.wait(te, "ve", up_rel[st_i * 2 + tb2])
                                for kc in range(8):
                                    mm = te.matmul(
                                        bank[:],
                                        wsb[par][wn][:, kc, 128 * jc:128 * (jc + 1)],
                                        xn_fm[:, kc, 1024 * tbp + 512 * tb2:
                                              1024 * tbp + 512 * (tb2 + 1)],
                                        start=(kc == 0), stop=(kc == 7))
                                p.inc(mm, "pe")
                        pev = p.val("pe")
                        p.wait(v, "pe", pev)
                        p.wait(v, "w", bl)
                        colg = 2 * sl8 + jc
                        bca = bsb[epar][ba][:, colg:colg + 1]
                        bcb = bsb[epar][bb][:, colg:colg + 1]
                        for tb2 in range(2):
                            sl2 = slice(512 * tb2, 512 * (tb2 + 1))
                            v.tensor_scalar(t_w3[:, sl2], ps_up[2 + tb2][:], bcb,
                                            None, op0=ALU.add)
                            up_rel[2 + tb2] = p.val("ve") + 1
                            vv = p.inc(v.scalar_tensor_tensor(
                                scr_g[:, sl2], ps_up[tb2][:], bca, t_w3[:, sl2],
                                op0=ALU.add, op1=ALU.mult), "ve")
                            up_rel[tb2] = vv
                        if pas == "g":
                            v.tensor_scalar(scr_g[:], scr_g[:], 7.0, None, op0=ALU.min)
                            vv = p.incd(v, "ve")
                            p.wait(s, "ve", vv)
                            sgv = p.inc(s.activation(scr_s[:], scr_g[:], AF.Sigmoid,
                                                     scale=1.702), "se")
                            p.wait(v, "se", sgv)
                            p.inc(v.tensor_tensor(gs_sb[:, jc, :], scr_g[:],
                                                  scr_s[:], op=ALU.mult), "ve")
                        else:
                            v.tensor_scalar(scr_g[:], scr_g[:], 7.0, -7.0,
                                            op0=ALU.min, op1=ALU.max)
                            v.tensor_scalar(scr_g[:], scr_g[:], 1.0, None, op0=ALU.add)
                            v.tensor_tensor(scr_s[:], gs_sb[:, jc, :], scr_g[:],
                                            op=ALU.mult)
                            p.wait(v, "pe", h_rel)
                            v.tensor_tensor(
                                h_sb[:, jc, :], scr_s[:],
                                comb_bc.ap().rearrange("p a b -> p (a b)")
                                [:, 1024 * tbp:1024 * (tbp + 1)], op=ALU.mult)
                            p.incd(v, "ve")
                            cbc_rel = p.val("ve")
                hv = p.val("ve")
                p.wait(te, "ve", hv)
                for dc in range(8):
                    for tb2 in range(2):
                        bi = (dc * 2 + tb2) % 2
                        bank = ps_dn[bi]
                        p.wait(te, "ve", dn_rel[bi])
                        for ic in range(2):
                            mm = te.matmul(bank[:],
                                           w2sb[par][:, ic, 128 * dc:128 * (dc + 1)],
                                           h_sb[:, ic, 512 * tb2:512 * (tb2 + 1)],
                                           start=(ic == 0),
                                           stop=(ic == 1 and sl8 != 0))
                        if sl8 == 0:
                            p.wait(te, "w", bl)
                            mm = te.matmul(bank[:],
                                           b2r[epar][0:1, 128 * dc:128 * (dc + 1)],
                                           crow[epar][0:1,
                                                      1024 * tbp + 512 * tb2:
                                                      1024 * tbp + 512 * (tb2 + 1)],
                                           start=False, stop=True)
                        pev = p.inc(mm, "pe")
                        p.wait(v, "pe", pev)
                        gsl = g0[:, dc,
                                 1024 * tbp + 512 * tb2:1024 * tbp + 512 * (tb2 + 1)]
                        if e == 0 and sl8 == 0:
                            vv = p.inc(v.tensor_copy(gsl, bank[:]), "ve")
                        else:
                            vv = p.inc(v.scalar_tensor_tensor(
                                gsl, bank[:], 1.0, gsl, op0=ALU.mult, op1=ALU.add), "ve")
                        dn_rel[bi] = vv
                h_rel = p.val("pe")
            wbuf_free[par] = p.val("pe")
            if sl8 == NSLAB - 1:
                bbuf_free[epar] = (p.val("pe"), p.val("se"))

    p.incd(v, "ve")
    for cl in reversed(ffn_scoped):
        cl()
    p.fence_loads()
    s0_sb, s0_cl = p.sbuf_scoped("s0_sb", [128, 8, 512], BF16)
    stage, stage_cl = p.sbuf_scoped("stage", [128, 8, 512], F32)
    sil_bf, sil_cl = p.sbuf_scoped("sil_bf", [128, 512], BF16)
    scr_f, scrf_cl = p.sbuf_scoped("scr_f", [128, 512], F32)
    xrel = [0, 0]
    strel = 0
    s0rel = 0
    for tbb in range(4):
        for q in range(4):
            t = 4 * tbb + q
            par = t % 2
            p.wait(sy, "ve", xrel[par])
            ldv = p.inc(sy.dma_start(out=x_tile[par][:],
                                     in_=xB[128 * t:128 * (t + 1), :]), "w", 16)
            for half in range(2):
                p.wait(te, "w", ldv)
                p.wait(te, "ve", strel)
                for ch in range(4):
                    c4 = 4 * half + ch
                    mm = te.transpose(ps_trf[:, 128 * ch:128 * (ch + 1)],
                                      x_tile[par][:, 128 * c4:128 * (c4 + 1)], identF[:])
                pev = p.inc(mm, "pe")
                p.wait(v, "pe", pev)
                strel = p.inc(v.tensor_copy(
                    stage[:, 4 * half:4 * half + 4, 128 * q:128 * (q + 1)],
                    ps_trf[:, 0:512].rearrange("p (c f) -> p c f", c=4)), "ve")
                xrel[par] = strel
        p.wait(sy, "ve", s0rel)
        s0v = p.inc(sy.dma_start(out=s0_sb[:],
                                 in_=s0_d[:, :, 512 * tbb:512 * (tbb + 1)]), "w", 16)
        for dc in range(8):
            sl = slice(512 * tbb, 512 * (tbb + 1))
            p.wait(s, "ve", p.val("ve"))
            s.activation(sil_bf[:], g0[:, dc, sl], AF.Silu)
            sev = p.incd(s, "se")
            p.wait(v, "se", sev)
            p.wait(v, "w", s0v)
            v.tensor_tensor(scr_f[:], s0_sb[:, dc, :], sil_bf[:], op=ALU.mult)
            v.tensor_tensor(g0[:, dc, sl], scr_f[:], stage[:, dc, :], op=ALU.add)
            s0rel = p.incd(v, "ve")
    p.wait(sy, "ve", p.val("ve"))
    p.inc(sy.dma_start(out=x1B, in_=g0[:]), "st", 16)
    p.wait(sy, "st", p.val("st"))
    scrf_cl(); sil_cl(); stage_cl(); s0_cl()


_progB = None


def run_launchB(x, preWoParts, pr):
    global _progB
    if _progB is None:
        _progB = build_launchB()
    base = dict(
        WoT=pr["WoT"], gwT=pr["gwT0"], gb=pr["gb0"],
        w1g=pr["w1g"], w1l=pr["w1l"], w3g=pr["w3g"], w3l=pr["w3l"],
        sw1g=pr["sw1g"], sw1l=pr["sw1l"], sw3g=pr["sw3g"], sw3l=pr["sw3l"],
        w2=pr["w2"], sw2=pr["sw2"],
        b1g=pr["b1g"], b1l=pr["b1l"], b3g=pr["b3g"], b3l=pr["b3l"],
        sb1g=pr["sb1g"], sb1l=pr["sb1l"], sb3g=pr["sb3g"], sb3l=pr["sb3l"],
        b2=pr["b2"], sb2=pr["sb2"])
    in_maps = []
    for c in range(8):
        b, half = c // 2, c % 2
        sl = slice(TOK * half, TOK * (half + 1))
        pw = np.concatenate([preWoParts[2 * b][:, :, sl],
                             preWoParts[2 * b + 1][:, :, sl]], axis=1)
        im = dict(base)
        im["xB"] = np.ascontiguousarray(x[b, sl])
        im["preWoB"] = np.ascontiguousarray(pw)
        in_maps.append(im)
    res = run_bass_kernel_spmd(_progB, in_maps, core_ids=list(range(8)))
    return [r["x1B"] for r in res.results]


def prep_weights(inp):
    pr = {}
    asc = np.ascontiguousarray
    pr["wqkv"] = asc(np.transpose(inp["L0_attn_W_qkv"], (0, 3, 1, 2)).reshape(3, D, H * DK))
    pr["WoT"] = asc(inp["L0_attn_W_o"].T)
    pr["gwT0"] = asc(inp["L0_ffn_gate_w"].T)
    pr["gb0"] = asc(np.asarray(inp["L0_ffn_gate_b"])[None, :])
    for nm, w in (("w1", inp["L0_ffn_w1"]), ("w3", inp["L0_ffn_w3"])):
        pr[nm + "g"] = asc(np.asarray(w)[:, :, 0::2])
        pr[nm + "l"] = asc(np.asarray(w)[:, :, 1::2])
    pr["w2"] = asc(inp["L0_ffn_w2"])
    for nm, b in (("b1", inp["L0_ffn_b1"]), ("b3", inp["L0_ffn_b3"])):
        b = np.asarray(b)
        pr[nm + "g"] = asc(b[:, 0::2].reshape(8, 16, 128).transpose(0, 2, 1))
        pr[nm + "l"] = asc(b[:, 1::2].reshape(8, 16, 128).transpose(0, 2, 1))
    pr["b2"] = asc(inp["L0_ffn_b2"])
    for nm, w in (("sw1", inp["L0_ffn_sw1"]), ("sw3", inp["L0_ffn_sw3"])):
        w = np.asarray(w)
        pr[nm + "g"] = asc(w[:, 0::2])
        pr[nm + "l"] = asc(w[:, 1::2])
    pr["sw2"] = asc(inp["L0_ffn_sw2"])
    for nm, b in (("sb1", inp["L0_ffn_sb1"]), ("sb3", inp["L0_ffn_sb3"])):
        b = np.asarray(b)
        pr[nm + "g"] = asc(b[0::2].reshape(16, 128).T)
        pr[nm + "l"] = asc(b[1::2].reshape(16, 128).T)
    pr["sb2"] = asc(np.asarray(inp["L0_ffn_sb2"])[None, :])
    pr["wqT"] = asc(inp["L1_attn_wq"].T)
    wkvT = np.asarray(inp["L1_attn_wkv"]).T.reshape(D, 16, 2, 64)
    pr["wkT1"] = asc(wkvT[:, :, 0, :].reshape(D, 1024))
    pr["wvT1"] = asc(wkvT[:, :, 1, :].reshape(D, 1024))
    pr["woT1"] = asc(inp["L1_attn_wo"].T)
    pr["gamma"] = asc(np.asarray(inp["L1_attn_gamma"]).reshape(8, 128).T)
    pr["beta"] = asc(np.asarray(inp["L1_attn_beta"]).reshape(8, 128).T)
    pr["gwT1"] = asc(inp["L1_ffn_gate_w"].T)
    pr["egT"] = asc(np.transpose(inp["L1_ffn_expert_gate"], (0, 2, 1)))
    pr["euT"] = asc(np.transpose(inp["L1_ffn_expert_up"], (0, 2, 1)))
    pr["edT"] = asc(np.transpose(inp["L1_ffn_expert_down"], (0, 2, 1)))
    pr["sgT"] = asc(inp["L1_ffn_shared_gate"].T)
    pr["suT"] = asc(inp["L1_ffn_shared_up"].T)
    pr["sdT"] = asc(inp["L1_ffn_shared_down"].T)
    return pr


EXT = TOK + 128


def emit_ln_fm(p, src_fm, nchunk_tok, xout_cb, blocks, bufs, gamma=None, beta=None):
    nc = p.nc
    v, s, te = nc.vector, nc.scalar, nc.tensor
    xsq2, rows, onesc, onesr = bufs["xsq"], bufs["rows"], bufs["onesc"], bufs["onesr"]
    ps_r, ps_b1, ps_b2, scr = bufs["ps_r"], bufs["ps_b1"], bufs["ps_b2"], bufs["scr"]
    sq_rel = [0, 0]
    row_rel = 0
    b_rel = 0
    src_ready = p.val("ve")
    p.wait(s, "ve", src_ready)
    p.wait(te, "ve", src_ready)
    for (st0, w_) in blocks:
        sl = slice(st0, st0 + w_)
        p.wait(te, "ve", max(row_rel, b_rel))
        for kc in range(8):
            par = kc % 2
            xsq = xsq2[par]
            p.wait(s, "pe", sq_rel[par])
            s.square(xsq[:, 0:w_], src_fm[:, kc, sl])
            sqv = p.incd(s, "se")
            mm = te.matmul(ps_r[0:1, 0:w_], onesc[:], src_fm[:, kc, sl],
                           start=(kc == 0), stop=(kc == 7))
            p.wait(te, "se", sqv)
            mm = te.matmul(ps_b1[0:1, 0:w_], onesc[:], xsq[:, 0:w_],
                           start=(kc == 0), stop=(kc == 7))
            sq_rel[par] = p.inc(mm, "pe")
        pev = p.val("pe")
        p.wait(v, "pe", pev)
        v.tensor_scalar(rows[0][0:1, 0:w_], ps_r[0:1, 0:w_], 1.0 / 1024, None,
                        op0=ALU.mult)
        v.tensor_scalar(rows[1][0:1, 0:w_], ps_b1[0:1, 0:w_], 1.0 / 1024, None,
                        op0=ALU.mult)
        v.drain()
        v.tensor_tensor(rows[2][0:1, 0:w_], rows[0][0:1, 0:w_], rows[0][0:1, 0:w_],
                        op=ALU.mult)
        v.drain()
        v.tensor_tensor(rows[3][0:1, 0:w_], rows[1][0:1, 0:w_], rows[2][0:1, 0:w_],
                        op=ALU.subtract)
        v.tensor_scalar(rows[3][0:1, 0:w_], rows[3][0:1, 0:w_], LN_EPS, None,
                        op0=ALU.add)
        row_rel = vv = p.incd(v, "ve")
        p.wait(s, "ve", vv)
        s.activation(rows[4][0:1, 0:w_], rows[3][0:1, 0:w_], AF.Sqrt)
        sv = p.incd(s, "se")
        p.wait(v, "se", sv)
        v.reciprocal(rows[5][0:1, 0:w_], rows[4][0:1, 0:w_])
        rv = p.incd(v, "ve")
        p.wait(te, "ve", max(rv, b_rel))
        mm1 = te.matmul(ps_b1[:, 0:w_], onesr[:], rows[0][0:1, 0:w_],
                        start=True, stop=True)
        mm2 = te.matmul(ps_b2[:, 0:w_], onesr[:], rows[5][0:1, 0:w_],
                        start=True, stop=True)
        bv = p.inc(mm2, "pe")
        p.wait(v, "pe", bv)
        for kc in range(8):
            v.tensor_tensor(scr[:, 0:w_], src_fm[:, kc, sl], ps_b1[:, 0:w_],
                            op=ALU.subtract)
            if gamma is None:
                xout_cb(kc, sl, v, scr, ps_b2, w_)
            else:
                v.tensor_tensor(scr[:, 0:w_], scr[:, 0:w_], ps_b2[:, 0:w_],
                                op=ALU.mult)
                xout_cb(kc, sl, v, scr, None, w_)
        b_rel = p.incd(v, "ve")
    return b_rel


def declare_C_weights(nc):
    d = {}
    d["wqT_d"] = nc.dram_tensor("wqT", [D, D], F32, kind="ExternalInput").ap()
    d["wkT_d"] = nc.dram_tensor("wkT1", [D, D], F32, kind="ExternalInput").ap()
    d["wvT_d"] = nc.dram_tensor("wvT1", [D, D], F32, kind="ExternalInput").ap()
    d["woT_d"] = nc.dram_tensor("woT1", [D, D], F32, kind="ExternalInput").ap()
    d["gam_d"] = nc.dram_tensor("gamma", [128, 8], F32, kind="ExternalInput").ap()
    d["bet_d"] = nc.dram_tensor("beta", [128, 8], F32, kind="ExternalInput").ap()
    return d


def build_launchC(stop=None):
    p = Prog()
    nc = p.nc
    x1C = nc.dram_tensor("x1C", [128, 8, EXT], F32, kind="ExternalInput").ap()
    dw = declare_C_weights(nc)
    s1C = nc.dram_tensor("s1C", [128, 8, TOK], BF16, kind="ExternalOutput").ap()
    xn1C = nc.dram_tensor("xn1C", [128, 8, TOK], BF16, kind="ExternalOutput").ap()
    sc = Scope(p, "C_")
    emit_C(p, sc, None, x1C, dw, s1C, xn1C, stop=stop)
    sc.close()
    return p.finish()


def emit_C(p, sc, x1_ld, x1C, dw, s1C, xn1C, stop=None):
    nc = p.nc
    v, s, te, sy, g = nc.vector, nc.scalar, nc.tensor, nc.sync, nc.gpsimd
    C = _consts()
    wqT_d, wkT_d, wvT_d = dw["wqT_d"], dw["wkT_d"], dw["wvT_d"]
    woT_d, gam_d, bet_d = dw["woT_d"], dw["gam_d"], dw["bet_d"]
    ident = sc.sbuf("identS", [128, 128], BF16)
    onesr = sc.sbuf("onesr", [1, 128], F32)
    onesc = sc.sbuf("onesc", [128, 1], F32)
    bandL = sc.sbuf("bandL", [128, 128], BF16)
    bandR = sc.sbuf("bandR", [128, 128], BF16)
    gam = sc.sbuf("gam", [128, 8], F32)
    bet = sc.sbuf("bet", [128, 8], F32)
    for nm, sb, cc in (("ident", ident, C["ident"]), ("onesr", onesr, C["ones_row"]),
                       ("onesc", onesc, np.ones((128, 1), np.float32)),
                       ("bandL", bandL, C["bandL"]), ("bandR", bandR, C["bandR"])):
        p.inc(sy.dma_start(out=sb[:], in_=p.const(nm + "_c", cc)), "w", 16)
    p.inc(sy.dma_start(out=gam[:], in_=gam_d), "w", 16)
    p.inc(sy.dma_start(out=bet[:], in_=bet_d), "w", 16)

    xn1 = sc.sbuf("xn1", [128, 8, EXT], BF16)
    rows = [sc.sbuf(f"row{i}", [1, 512], F32) for i in range(6)]
    xsq2 = [sc.sbuf(f"xsq{i}", [128, 512], F32) for i in range(2)]
    scr = sc.sbuf("scrC", [128, 512], F32)
    ps_r = sc.psum("ps_r", [128, 512], F32)
    ps_b1 = sc.psum("ps_b1", [128, 512], F32)
    ps_b2 = sc.psum("ps_b2", [128, 512], F32)
    ps_p = sc.psum("ps_p", [128, 512], F32)
    ps_st = sc.psum("ps_st", [128, 512], F32)
    ps_av = sc.psum("ps_av", [128, 512], F32)
    ps_tr = sc.psum("ps_trC", [128, 1024], BF16)

    lnbufs = dict(xsq=xsq2, rows=rows, onesc=onesc[:], onesr=onesr[:],
                  ps_r=ps_r, ps_b1=ps_b1, ps_b2=ps_b2, scr=scr)

    x1sb, x1_cl = p.sbuf_scoped("x1sb", [128, 8, EXT], F32)
    if x1_ld is None:
        p.inc(sy.dma_start(out=x1sb[:], in_=x1C), "w", 16)
    else:
        x1_ld(x1sb)
    p.wait(s, "w", p.val("w"))
    p.wait(te, "w", p.val("w"))
    p.wait(v, "w", p.val("w"))
    blocks = [(0, 512), (512, 512), (1024, 512), (1536, 512), (2048, 128)]

    def ln1_out(kc, sl, v_, scr_, ps_rstd, w_):
        v_.tensor_tensor(xn1[:, kc, sl], scr_[:, 0:w_], ps_rstd[:, 0:w_],
                         op=ALU.mult)
    emit_ln_fm(p, x1sb, 8, ln1_out, blocks, lnbufs)
    p.incd(v, "ve")
    x1_cl()
    p.fence_loads()
    if stop == "ln":
        p.wait(sy, "ve", p.val("ve"))
        p.inc(sy.dma_start(out=xn1C, in_=xn1[:, :, 128:EXT]), "st", 16)
        p.inc(sy.dma_start(out=s1C, in_=xn1[:, :, 128:EXT]), "st", 16)
        p.wait(sy, "st", p.val("st"))
        return p.finish()

    msg_fm = sc.sbuf("msg_fm", [128, 8, TOK], BF16)
    att_scoped = []

    def asb(name, shape, dtype):
        h, cl = p.sbuf_scoped(name, shape, dtype)
        att_scoped.append(cl)
        return h
    w_sb = [[asb(f"wC{i}{j}", [128, 8, 128], BF16) for j in range(3)]
            for i in range(2)]
    k_fm = asb("k_fm", [128, EXT], BF16)
    q_fm = asb("q_fm", [128, TOK], BF16)
    vtokC = asb("vtokC", [128, 2, 17, 64], BF16)
    evacC = asb("evacC", [128, 512], BF16)
    evB = asb("evBC", [64, 512], BF16)
    tau = asb("tau", [128, 256], BF16)

    wld = [[0, 0, 0], [0, 0, 0]]
    wfree = [0, 0]

    def loadw(pair, par):
        p.wait(g, "pe", wfree[par])
        srcs = (wqT_d[:, 128 * pair:128 * (pair + 1)],
                wkT_d[:, 128 * pair:128 * (pair + 1)],
                wvT_d[:, 128 * pair:128 * (pair + 1)])
        for j, src in enumerate(srcs):
            wld[par][j] = p.inc(g.dma_start(
                out=w_sb[par][j][:], in_=src.rearrange("(c p) m -> p c m", p=128)),
                "w", 16)

    loadw(0, 0)
    loadw(1, 1)
    NBE = EXT // 512 + (1 if EXT % 512 else 0)
    eblocks = [(512 * i, min(512, EXT - 512 * i)) for i in range(5)]
    rel_proj = 0
    rel_tr = 0
    for pair in range(8):
        par = pair % 2
        if pair >= 2:
            loadw(pair, par)
        wq, wk, wv = w_sb[par]
        pair_gate = p.val("pe")
        for which in ("k", "q"):
            wmat = wk if which == "k" else wq
            blks = eblocks if which == "k" else [(128 + 512 * i, 512) for i in range(4)]
            for (st0, w_) in blks:
                sl = slice(st0, st0 + w_)
                p.wait(te, "w", wld[par][1 if which == "k" else 0])
                p.wait(te, "se", rel_proj)
                for kc in range(8):
                    mm = te.matmul(ps_p[:, 0:w_], wmat[:, kc, :], xn1[:, kc, sl],
                                   start=(kc == 0), stop=(kc == 7))
                pev = p.inc(mm, "pe")
                p.wait(s, "pe", max(pev, pair_gate))
                if which == "k":
                    s.copy(k_fm[:, sl], ps_p[:, 0:w_])
                else:
                    s.copy(q_fm[:, st0 - 128:st0 - 128 + w_], ps_p[:, 0:w_])
                rel_proj = p.incd(s, "se")
        for (st0, w_) in eblocks:
            sl = slice(st0, st0 + w_)
            nt = w_ // 128
            p.wait(te, "w", wld[par][2])
            p.wait(te, "se", rel_proj)
            for kc in range(8):
                mm = te.matmul(ps_p[:, 0:w_], wv[:, kc, :], xn1[:, kc, sl],
                               start=(kc == 0), stop=(kc == 7))
            pev = p.inc(mm, "pe")
            p.wait(s, "pe", max(pev, pair_gate))
            s.copy(evacC[:, 0:w_], ps_p[:, 0:w_])
            rel_proj = sev = p.incd(s, "se")
            p.wait(v, "se", sev)
            v.tensor_copy(evB[:, 0:w_], evacC[64:128, 0:w_])
            bev = p.incd(v, "ve")
            p.wait(te, "ve", max(bev, rel_tr))
            for h2 in range(2):
                for j4 in range(nt):
                    src = evacC[0:64, 128 * j4:128 * (j4 + 1)] if h2 == 0 else \
                        evB[:, 128 * j4:128 * (j4 + 1)]
                    mm = te.transpose(ps_tr[:, h2 * 256 + 64 * j4:h2 * 256 + 64 * (j4 + 1)],
                                      src, ident[0:64, 0:64])
            pev = p.inc(mm, "pe")
            p.wait(v, "pe", pev)
            t0 = st0 // 128
            src3 = ps_tr[:, 0:512].rearrange("p (two j f) -> p two j f", two=2, j=4)
            for h2 in range(2):
                v.tensor_copy(vtokC[:, h2, t0:t0 + nt, :], src3[:, h2, 0:nt, :])
            rel_tr = p.incd(v, "ve")
        wfree[par] = p.val("pe")
        if stop == "proj":
            break

        rel_st = 0
        rel_avv = 0
        for h2 in range(2):
            hb = 64 * h2
            for t in range(16):
                p.wait(te, "se", rel_st)
                mmL = te.matmul(ps_st[:, 0:128], k_fm[hb:hb + 64, 128 * t:128 * (t + 1)],
                                q_fm[hb:hb + 64, 128 * t:128 * (t + 1)],
                                start=True, stop=True)
                mmR = te.matmul(ps_st[:, 128:256],
                                k_fm[hb:hb + 64, 128 * (t + 1):128 * (t + 2)],
                                q_fm[hb:hb + 64, 128 * t:128 * (t + 1)],
                                start=True, stop=True)
                pev = p.inc(mmR, "pe")
                p.wait(s, "pe", pev)
                s.activation(tau[:, 0:128], ps_st[:, 0:128], AF.Sigmoid, scale=0.125)
                s.activation(tau[:, 128:256], ps_st[:, 128:256], AF.Sigmoid, scale=0.125)
                rel_st = sev = p.incd(s, "se")
                p.wait(v, "se", sev)
                v.tensor_tensor(tau[:, 0:128], tau[:, 0:128], bandL[:], op=ALU.mult)
                v.tensor_tensor(tau[:, 128:256], tau[:, 128:256], bandR[:], op=ALU.mult)
                tv = p.incd(v, "ve")
                p.wait(te, "ve", max(tv, rel_avv))
                for kt in range(2):
                    mm = te.matmul(ps_av[0:64, 0:128], vtokC[:, h2, t + kt, :],
                                   tau[:, 128 * kt:128 * (kt + 1)],
                                   start=(kt == 0), stop=(kt == 1))
                pev = p.inc(mm, "pe")
                p.wait(v, "pe", pev)
                v.tensor_copy(msg_fm[hb:hb + 64, pair, 128 * t:128 * (t + 1)],
                              ps_av[0:64, 0:128])
                rel_avv = p.incd(v, "ve")

    if stop in ("proj", "attn"):
        p.wait(sy, "ve", p.val("ve"))
        p.wait(sy, "se", p.val("se"))
        p.inc(sy.dma_start(out=xn1C, in_=xn1[:, :, 128:EXT]), "st", 16)
        if stop == "proj":
            p.inc(sy.dma_start(out=s1C[:, 0, 0:TOK], in_=k_fm[:, 0:TOK]), "st", 16)
            p.inc(sy.dma_start(out=s1C[:, 1, 0:TOK], in_=q_fm[:]), "st", 16)
            for h2x in range(2):
                p.inc(sy.dma_start(out=s1C[:, 2 + h2x, 0:17 * 64],
                                   in_=vtokC[:, h2x, :, :]), "st", 16)
        p.inc(sy.dma_start(out=s1C, in_=msg_fm[:]), "st", 16) if stop != "proj" else None
        p.wait(sy, "st", p.val("st"))
        for cl in reversed(att_scoped):
            cl()
        return p.finish()
    for cl in reversed(att_scoped):
        cl()
    p.fence_loads()
    y_sb = sc.sbuf("y_sb", [128, 8, TOK], F32)
    woT, woT_cl = p.sbuf_scoped("woT1s", [128, 8, D], BF16)
    p.inc(g.dma_start(out=woT[:], in_=woT_d.rearrange("(c p) m -> p c m", p=128)),
          "w", 16)
    p.wait(te, "w", p.val("w"))
    p.wait(te, "ve", p.val("ve"))
    rel_p = 0
    for dc in range(8):
        for tbb in range(4):
            sl = slice(512 * tbb, 512 * (tbb + 1))
            esl = slice(128 + 512 * tbb, 128 + 512 * (tbb + 1))
            p.wait(te, "ve", rel_p)
            for hc in range(8):
                mm = te.matmul(ps_p[:], woT[:, hc, 128 * dc:128 * (dc + 1)],
                               msg_fm[:, hc, sl], start=(hc == 0), stop=(hc == 7))
            pev = p.inc(mm, "pe")
            p.wait(v, "pe", pev)
            v.tensor_tensor(y_sb[:, dc, sl], ps_p[:], xn1[:, dc, esl], op=ALU.add)
            rel_p = p.incd(v, "ve")
    woT_cl()

    oblocks = [(512 * i, 512) for i in range(4)]
    s1_sb = sc.sbuf("s1_sb", [128, 8, TOK], BF16)

    def s1_out(kc, sl, v_, scr_, _unused, w_):
        v_.tensor_scalar(s1_sb[:, kc, sl], scr_[:, 0:w_], gam[:, kc:kc + 1],
                         bet[:, kc:kc + 1], op0=ALU.mult, op1=ALU.add)
    emit_ln_fm(p, y_sb, 8, s1_out, oblocks, lnbufs, gamma=gam, beta=bet)
    p.incd(v, "ve")
    p.wait(sy, "ve", p.val("ve"))
    p.inc(sy.dma_start(out=s1C, in_=s1_sb[:]), "st", 16)
    p.inc(sy.dma_start(out=xn1C, in_=xn1[:, :, 128:EXT]), "st", 16)
    p.wait(sy, "st", p.val("st"))
_progC = None


def run_launchC(x1parts, pr):
    global _progC
    if _progC is None:
        _progC = build_launchC()
    in_maps = []
    for c in range(8):
        b, half = c // 2, c % 2
        own = np.asarray(x1parts[c])
        if half == 0:
            halo = np.zeros((128, 8, 128), np.float32)
        else:
            halo = np.asarray(x1parts[2 * b])[:, :, TOK - 128:TOK]
        x1e = np.concatenate([halo, own], axis=2)
        in_maps.append(dict(x1C=np.ascontiguousarray(x1e), wqT=pr["wqT"],
                            wkT1=pr["wkT1"], wvT1=pr["wvT1"], woT1=pr["woT1"],
                            gamma=pr["gamma"], beta=pr["beta"]))
    res = run_bass_kernel_spmd(_progC, in_maps, core_ids=list(range(8)))
    return ([r["s1C"] for r in res.results], [r["xn1C"] for r in res.results])


def declare_D_weights(nc):
    I1 = 2816
    d = {}
    d["gwT_d"] = nc.dram_tensor("gwT1", [D, 8], F32, kind="ExternalInput").ap()
    d["egT_d"] = nc.dram_tensor("egT", [8, D, I1], F32, kind="ExternalInput").ap()
    d["euT_d"] = nc.dram_tensor("euT", [8, D, I1], F32, kind="ExternalInput").ap()
    d["edT_d"] = nc.dram_tensor("edT", [8, I1, D], F32, kind="ExternalInput").ap()
    d["sgT_d"] = nc.dram_tensor("sgT", [D, I1], F32, kind="ExternalInput").ap()
    d["suT_d"] = nc.dram_tensor("suT", [D, I1], F32, kind="ExternalInput").ap()
    d["sdT_d"] = nc.dram_tensor("sdT", [I1, D], F32, kind="ExternalInput").ap()
    return d


def build_launchD():
    p = Prog()
    nc = p.nc
    xn1D = nc.dram_tensor("xn1D", [128, 8, TOK], BF16, kind="ExternalInput").ap()
    s1D = nc.dram_tensor("s1D", [128, 8, TOK], BF16, kind="ExternalInput").ap()
    x1D = nc.dram_tensor("x1D", [128, 8, TOK], F32, kind="ExternalInput").ap()
    dw = declare_D_weights(nc)
    outD = nc.dram_tensor("outD", [TOK, D], F32, kind="ExternalOutput").ap()
    comb_d = nc.dram_tensor("combscrD", [8, TOK], F32, kind="Internal").ap()
    sc = Scope(p, "D_")
    emit_D(p, sc, xn1D, s1D, x1D, dw, outD, comb_d)
    sc.close()
    return p.finish()


def emit_D(p, sc, xn1D, s1D, x1D, dw, outD, comb_d):
    I1 = 2816
    nc = p.nc
    v, s, te, sy, g = nc.vector, nc.scalar, nc.tensor, nc.sync, nc.gpsimd
    C = _consts()
    gwT_d, egT_d, euT_d, edT_d = dw["gwT_d"], dw["egT_d"], dw["euT_d"], dw["edT_d"]
    sgT_d, suT_d, sdT_d = dw["sgT_d"], dw["suT_d"], dw["sdT_d"]
    ident = sc.sbuf("identS", [128, 128], BF16)
    identF = sc.sbuf("identF", [128, 128], F32)
    onesr = sc.sbuf("onesr", [1, 128], F32)
    p.inc(sy.dma_start(out=ident[:], in_=p.const("ident_c", C["ident"])), "w", 16)
    p.inc(sy.dma_start(out=identF[:], in_=p.const("identF_c", np.eye(128, dtype=np.float32))), "w", 16)
    p.inc(sy.dma_start(out=onesr[:], in_=p.const("onesr_c", C["ones_row"])), "w", 16)

    xn_fm = sc.sbuf("xn_fm", [128, 8, TOK], BF16)
    stats = sc.sbuf("stats", [128, 8], F32)
    ps_up = [sc.psum(f"ps_up{i}", [128, 512], F32) for i in range(4)]
    ps_dn = [sc.psum(f"ps_dn{i}", [128, 512], F32) for i in range(2)]
    ps_trf = sc.psum("ps_trf", [128, 512], F32)
    ldxn = p.inc(sy.dma_start(out=xn_fm[:], in_=xn1D), "w", 16)

    gwT, gwT_cl = p.sbuf_scoped("gwT_s", [128, 8, 8], BF16)
    sc_sb, sc_cl = p.sbuf_scoped("sc_sb", [128, 8], F32)
    cmp8, cmp8_cl = p.sbuf_scoped("cmp8", [128, 8, 8], F32)
    rank, rank_cl = p.sbuf_scoped("rank", [128, 8], F32)
    comb_tm, ctm_cl = p.sbuf_scoped("comb_tm", [128, 16, 8], F32)
    gstat, gstat_cl = p.sbuf_scoped("gstat", [128, 4], F32)
    p.inc(g.dma_start(out=gwT[:], in_=gwT_d.rearrange("(c p) e -> p c e", p=128)), "w", 16)
    p.wait(te, "w", p.val("w"))
    rel_ve, rel_se = 0, 0
    for t in range(16):
        p.wait(te, "ve", rel_ve)
        p.wait(te, "se", rel_se)
        for kc in range(8):
            mm = te.matmul(ps_trf[:, 0:8], xn_fm[:, kc, 128 * t:128 * (t + 1)],
                           gwT[:, kc, :], start=(kc == 0), stop=(kc == 7))
        pev = p.inc(mm, "pe")
        p.wait(v, "pe", pev)
        v.tensor_reduce(gstat[:, 0:1], ps_trf[:, 0:8], axis=AX.X, op=ALU.max)
        v.drain()
        v.tensor_scalar(gstat[:, 1:2], gstat[:, 0:1], -1.0, None, op0=ALU.mult)
        vv = p.incd(v, "ve")
        rel_ve = vv
        p.wait(s, "ve", vv)
        s.activation(sc_sb[:], ps_trf[:, 0:8], AF.Exp, bias=gstat[:, 1:2])
        sev = p.incd(s, "se")
        rel_se = sev
        p.wait(v, "se", sev)
        v.tensor_reduce(gstat[:, 2:3], sc_sb[:], axis=AX.X, op=ALU.add)
        v.drain()
        v.reciprocal(gstat[:, 3:4], gstat[:, 2:3])
        v.drain()
        v.tensor_scalar(sc_sb[:], sc_sb[:], gstat[:, 3:4], None, op0=ALU.mult)
        v.drain()
        for e in range(8):
            v.tensor_scalar(cmp8[:, e, :], sc_sb[:], sc_sb[:, e:e + 1], None,
                            op0=ALU.is_gt)
        v.drain()
        v.tensor_reduce(rank[:], cmp8.ap(), axis=AX.X, op=ALU.add)
        v.drain()
        v.tensor_scalar(rank[:], rank[:], 2.0, None, op0=ALU.is_lt)
        v.drain()
        v.tensor_tensor(rank[:], rank[:], sc_sb[:], op=ALU.mult)
        v.drain()
        v.tensor_reduce(gstat[:, 0:1], rank[:], axis=AX.X, op=ALU.add)
        v.drain()
        v.tensor_scalar(gstat[:, 0:1], gstat[:, 0:1], 1e-20, None, op0=ALU.add)
        v.drain()
        v.reciprocal(gstat[:, 1:2], gstat[:, 0:1])
        v.drain()
        v.tensor_scalar(comb_tm[:, t, :], rank[:], gstat[:, 1:2], None, op0=ALU.mult)
        p.incd(v, "ve")
    combT, combT_cl = p.sbuf_scoped("combT", [8, TOK], F32)
    tr_rel = 0
    for t in range(16):
        p.wait(te, "ve", p.val("ve") if t == 0 else tr_rel)
        mm = te.transpose(ps_trf[0:8, 0:128], comb_tm[:, t, :], identF[:])
        pev = p.inc(mm, "pe")
        p.wait(v, "pe", pev)
        v.tensor_copy(combT[:, 128 * t:128 * (t + 1)], ps_trf[0:8, 0:128])
        tr_rel = p.incd(v, "ve")
    p.wait(sy, "ve", tr_rel)
    cdv = p.inc(sy.dma_start(out=comb_d, in_=combT[:]), "st", 16)
    p.wait(sy, "st", cdv)
    combT_cl(); gstat_cl(); ctm_cl(); rank_cl(); cmp8_cl(); sc_cl(); gwT_cl()
    p.fence_loads()

    g1 = sc.sbuf("g1", [128, 8, TOK], F32)
    ffn_scoped = []

    def fsb(name, shape, dtype):
        h, cl = p.sbuf_scoped(name, shape, dtype)
        ffn_scoped.append(cl)
        return h
    comb_bc = fsb("comb_bc", [128, 4, 512], F32)
    wg_sb = [fsb(f"wg_sb{i}", [128, 8, 256], BF16) for i in range(2)]
    wu_sb = [fsb(f"wu_sb{i}", [128, 8, 256], BF16) for i in range(2)]
    wd_sb = [fsb(f"wd_sb{i}", [128, 2, 1024], BF16) for i in range(2)]
    crow = [fsb(f"crow{i}", [1, TOK], F32) for i in range(2)]
    sgl = fsb("sgl", [128, 1024], F32)
    scr_u = fsb("scr_u", [128, 1024], F32)
    h_sb = fsb("h_sbD", [128, 2, 1024], BF16)

    NSLAB = 11
    wbuf_free = [0, 0]
    slab_ld = [0, 0]
    crow_ld = [0, 0]
    crow_rel = [(0, 0), (0, 0)]

    def load_slab(e, sl8, par):
        shared = (e == 8)
        p.wait(g, "pe", wbuf_free[par])
        for dstb, srcd, srcs in ((wg_sb, egT_d, sgT_d), (wu_sb, euT_d, suT_d)):
            src = srcs if shared else srcd[e]
            sli = src[:, 256 * sl8:256 * (sl8 + 1)]
            p.inc(g.dma_start(out=dstb[par][:],
                              in_=sli.rearrange("(c p) j -> p c j", p=128)), "w", 16)
        src = sdT_d if shared else edT_d[e]
        ld = p.inc(g.dma_start(out=wd_sb[par][:],
                               in_=src[256 * sl8:256 * (sl8 + 1), :]
                               .rearrange("(c p) d -> p c d", p=128)), "w", 16)
        slab_ld[par] = ld

    def load_crow(e, par):
        pe_r, ve_r = crow_rel[par]
        if e == 8:
            p.wait(v, "pe", pe_r)
            crow_ld[par] = None
            v.memset(crow[par][:], 1.0)
            p.incd(v, "ve")
        else:
            p.wait(sy, "pe", pe_r)
            p.wait(sy, "ve", ve_r)
            crow_ld[par] = p.inc(sy.dma_start(out=crow[par][:],
                                              in_=comb_d[e:e + 1, :]), "w", 16)

    load_slab(0, 0, 0)
    load_crow(0, 0)
    cbc_rel = 0
    dn_rel = [0, 0]
    up_rel = [0, 0, 0, 0]
    up_sem = ["se", "se", "ve", "ve"]
    h_rel = 0

    for e in range(9):
        epar = e % 2
        if e < 8:
            load_crow(e + 1, (e + 1) % 2)
        if crow_ld[epar]:
            p.wait(te, "w", crow_ld[epar])
        else:
            p.wait(te, "ve", p.val("ve"))
        for tbb in range(4):
            p.wait(te, "ve", dn_rel[0])
            mm = te.matmul(ps_dn[0][:], onesr[:],
                           crow[epar][0:1, 512 * tbb:512 * (tbb + 1)],
                           start=True, stop=True)
            pev = p.inc(mm, "pe")
            p.wait(v, "pe", pev)
            p.wait(v, "ve", cbc_rel)
            v.tensor_copy(comb_bc[:, tbb, :], ps_dn[0][:])
            dn_rel[0] = p.incd(v, "ve")
        for sl8 in range(NSLAB):
            par = (e * NSLAB + sl8) % 2
            nxt = e * NSLAB + sl8 + 1
            if nxt < 9 * NSLAB:
                load_slab(nxt // NSLAB, nxt % NSLAB, nxt % 2)
            wl = slab_ld[par]
            for tbp in range(2):
                for jc in range(2):
                    p.wait(te, "w", wl)
                    for st_i, wb in ((0, wg_sb), (1, wu_sb)):
                        for tb2 in range(2):
                            bank = ps_up[st_i * 2 + tb2]
                            p.wait(te, up_sem[st_i * 2 + tb2], up_rel[st_i * 2 + tb2])
                            for kc in range(8):
                                mm = te.matmul(
                                    bank[:], wb[par][:, kc, 128 * jc:128 * (jc + 1)],
                                    xn_fm[:, kc, 1024 * tbp + 512 * tb2:
                                          1024 * tbp + 512 * (tb2 + 1)],
                                    start=(kc == 0), stop=(kc == 7))
                            p.inc(mm, "pe")
                    pev = p.val("pe")
                    p.wait(s, "pe", pev)
                    p.wait(s, "ve", p.val("ve"))
                    for tb2 in range(2):
                        s.activation(sgl[:, 512 * tb2:512 * (tb2 + 1)],
                                     ps_up[tb2][:], AF.Silu)
                        up_rel[tb2] = p.incd(s, "se")
                    sev = p.val("se")
                    p.wait(v, "se", sev)
                    p.wait(v, "pe", h_rel)
                    for tb2 in range(2):
                        sl2 = slice(512 * tb2, 512 * (tb2 + 1))
                        v.tensor_tensor(scr_u[:, sl2], ps_up[2 + tb2][:],
                                        sgl[:, sl2], op=ALU.mult)
                        up_rel[2 + tb2] = p.val("ve") + 1
                        v.tensor_tensor(h_sb[:, jc, sl2], scr_u[:, sl2],
                                        comb_bc.ap().rearrange("p a b -> p (a b)")
                                        [:, 1024 * tbp + 512 * tb2:
                                         1024 * tbp + 512 * (tb2 + 1)], op=ALU.mult)
                        p.incd(v, "ve")
                    cbc_rel = p.val("ve")
                hv = p.val("ve")
                p.wait(te, "ve", hv)
                for dc in range(8):
                    for tb2 in range(2):
                        bi = (dc * 2 + tb2) % 2
                        bank = ps_dn[bi]
                        p.wait(te, "ve", dn_rel[bi])
                        for ic in range(2):
                            mm = te.matmul(bank[:],
                                           wd_sb[par][:, ic, 128 * dc:128 * (dc + 1)],
                                           h_sb[:, ic, 512 * tb2:512 * (tb2 + 1)],
                                           start=(ic == 0), stop=(ic == 1))
                        pev = p.inc(mm, "pe")
                        p.wait(v, "pe", pev)
                        gsl = g1[:, dc,
                                 1024 * tbp + 512 * tb2:1024 * tbp + 512 * (tb2 + 1)]
                        if e == 0 and sl8 == 0:
                            v.tensor_copy(gsl, bank[:])
                        else:
                            v.scalar_tensor_tensor(gsl, bank[:], 1.0, gsl,
                                                   op0=ALU.mult, op1=ALU.add)
                        dn_rel[bi] = p.inc(v.engine_nop(), "ve") if False else \
                            p.incd(v, "ve")
                h_rel = p.val("pe")
            wbuf_free[par] = p.val("pe")
        crow_rel[epar] = (p.val("pe"), p.val("ve"))

    p.incd(v, "ve")
    for cl in reversed(ffn_scoped):
        cl()
    p.fence_loads()

    x1_sb, x1_cl = p.sbuf_scoped("x1_sbD", [128, 8, 512], F32)
    s1_sb, s1_cl = p.sbuf_scoped("s1_sbD", [128, 8, 512], BF16)
    sil_bf, sil_cl = p.sbuf_scoped("sil_bfD", [128, 512], BF16)
    scr_f, scrf_cl = p.sbuf_scoped("scr_fD", [128, 512], F32)
    out_tm, otm_cl = p.sbuf_scoped("out_tm", [128, 1024], F32)
    ldrel = 0
    orel = 0
    st_rel = 0
    for tbb in range(4):
        sl = slice(512 * tbb, 512 * (tbb + 1))
        p.wait(sy, "ve", ldrel)
        l1 = p.inc(sy.dma_start(out=x1_sb[:], in_=x1D[:, :, sl]), "w", 16)
        l2 = p.inc(sy.dma_start(out=s1_sb[:], in_=s1D[:, :, sl]), "w", 16)
        for dc in range(8):
            p.wait(s, "ve", p.val("ve"))
            s.activation(sil_bf[:], g1[:, dc, sl], AF.Silu)
            sev = p.incd(s, "se")
            p.wait(v, "se", sev)
            p.wait(v, "w", l2)
            v.tensor_tensor(scr_f[:], s1_sb[:, dc, :], sil_bf[:], op=ALU.mult)
            p.wait(v, "w", l1)
            v.tensor_tensor(g1[:, dc, sl], scr_f[:], x1_sb[:, dc, :], op=ALU.add)
            ldrel = p.incd(v, "ve")
        for q in range(4):
            t = 4 * tbb + q
            for half in range(2):
                p.wait(te, "ve", max(ldrel, orel))
                for ch in range(4):
                    c4 = 4 * half + ch
                    mm = te.transpose(ps_trf[:, 128 * ch:128 * (ch + 1)],
                                      g1[:, 4 * half + ch if False else c4,
                                         128 * t:128 * (t + 1)], identF[:])
                pev = p.inc(mm, "pe")
                p.wait(v, "pe", pev)
                p.wait(v, "st", st_rel)
                v.tensor_copy(out_tm.ap().rearrange("p (c f) -> p c f", c=8)
                              [:, 4 * half:4 * half + 4, :].rearrange("p c f -> p (c f)")
                              if False else out_tm[:, 512 * half:512 * (half + 1)],
                              ps_trf[:, 0:512])
                orel = p.incd(v, "ve")
            p.wait(sy, "ve", orel)
            st_rel = p.inc(sy.dma_start(out=outD[128 * t:128 * (t + 1), :],
                                        in_=out_tm[:]), "st", 16)
    p.wait(sy, "st", p.val("st"))
    otm_cl(); scrf_cl(); sil_cl(); s1_cl(); x1_cl()


_progD = None


def run_launchD(xn1parts, s1parts, x1parts, pr):
    global _progD
    if _progD is None:
        _progD = build_launchD()
    base = dict(gwT1=pr["gwT1"], egT=pr["egT"], euT=pr["euT"], edT=pr["edT"],
                sgT=pr["sgT"], suT=pr["suT"], sdT=pr["sdT"])
    in_maps = []
    for c in range(8):
        im = dict(base)
        im["xn1D"] = np.ascontiguousarray(np.asarray(xn1parts[c]))
        im["s1D"] = np.ascontiguousarray(np.asarray(s1parts[c]))
        im["x1D"] = np.ascontiguousarray(np.asarray(x1parts[c]))
        in_maps.append(im)
    res = run_bass_kernel_spmd(_progD, in_maps, core_ids=list(range(8)))
    return [r["outD"] for r in res.results]


def kernel(**inputs):
    pr = prep_weights(inputs)
    x = np.ascontiguousarray(np.asarray(inputs["L0_attn_x"], np.float32))
    parts = run_launchA(x, pr["wqkv"])
    x1parts = run_launchB(x, parts, pr)
    s1parts, xn1parts = run_launchC(x1parts, pr)
    outparts = run_launchD(xn1parts, s1parts, x1parts, pr)
    out = np.empty((B, T, D), np.float32)
    for c in range(8):
        b, half = c // 2, c % 2
        out[b, TOK * half:TOK * (half + 1)] = np.asarray(outparts[c])
    return out


def build_mega():
    p = Prog()
    nc = p.nc
    v, s, te, sy, g = nc.vector, nc.scalar, nc.tensor, nc.sync, nc.gpsimd
    GROUPS = [[0, 1], [2, 3], [4, 5], [6, 7]]

    xA = nc.dram_tensor("xA", [T, D], F32, kind="ExternalInput").ap()
    xB = nc.dram_tensor("xB", [TOK, D], F32, kind="ExternalInput").ap()
    wqkvA = nc.dram_tensor("wqkvA", [3, D, 512], F32, kind="ExternalInput").ap()
    hsel = nc.dram_tensor("hsel", [128, 2], BF16, kind="ExternalInput").ap()
    hself = nc.dram_tensor("hself", [128, 2], F32, kind="ExternalInput").ap()
    dwB = declare_B_weights(nc)
    dwC = declare_C_weights(nc)
    dwD = declare_D_weights(nc)
    outD = nc.dram_tensor("outD", [TOK, D], F32, kind="ExternalOutput").ap()

    preWoX = nc.dram_tensor("preWoX", [128, 4, T], BF16, kind="Internal")
    preWoAG = nc.dram_tensor("preWoAG", [256, 4 * T], BF16, kind="Internal").ap()
    s0scr = nc.dram_tensor("s0scr", [128, 8, TOK], BF16, kind="Internal").ap()
    combB = nc.dram_tensor("combscr", [8, TOK], F32, kind="Internal").ap()
    combD = nc.dram_tensor("combscrD", [8, TOK], F32, kind="Internal").ap()
    x1X = nc.dram_tensor("x1X", [128, 8, TOK], F32, kind="Internal").ap()
    haloX = nc.dram_tensor("haloX", [128, 8 * 128], F32, kind="Internal").ap()
    haloAG = nc.dram_tensor("haloAG", [256, 8 * 128], F32, kind="Internal").ap()
    s1X = nc.dram_tensor("s1X", [128, 8, TOK], BF16, kind="Internal").ap()
    xn1X = nc.dram_tensor("xn1X", [128, 8, TOK], BF16, kind="Internal").ap()

    hsel_sb = p.sbuf("hsel_sb", [128, 2], BF16)
    hself_sb = p.sbuf("hself_sb", [128, 2], F32)
    p.inc(sy.dma_start(out=hsel_sb[:], in_=hsel), "w", 16)
    p.inc(sy.dma_start(out=hself_sb[:], in_=hself), "w", 16)

    scA = Scope(p, "A_")
    emit_A(p, scA, xA, wqkvA, preWoX.ap())
    scA.close()

    p.wait(g, "st", p.val("st"))
    p.inc(g.collective_compute(
        "AllGather", ALU.bypass, replica_groups=GROUPS,
        ins=[preWoX.ap().rearrange("p a t -> p (a t)").opt()],
        outs=[preWoAG.opt()]), "cc", 1)

    scB = Scope(p, "B_")

    def preWo_ld(preWo):
        hi_scr, hi_cl = p.sbuf_scoped("hi_scr", [128, TOK], BF16)
        p.wait(sy, "cc", p.val("cc"))
        for hc in range(8):
            rb = 128 * (hc // 4)
            cb = (hc % 4) * T
            l1 = p.inc(sy.dma_start(out=preWo[:, hc, :],
                                    in_=preWoAG[rb:rb + 128, cb:cb + TOK]), "w", 16)
            l2 = p.inc(sy.dma_start(out=hi_scr[:],
                                    in_=preWoAG[rb:rb + 128, cb + TOK:cb + 2 * TOK]),
                       "w", 16)
            p.wait(v, "w", l2)
            v.tensor_scalar(preWo[:, hc, :], preWo[:, hc, :], hself_sb[:, 0:1],
                            None, op0=ALU.mult)
            v.scalar_tensor_tensor(preWo[:, hc, :], hi_scr[:], hself_sb[:, 1:2],
                                   preWo[:, hc, :], op0=ALU.mult, op1=ALU.add)
            bv = p.incd(v, "ve")
        p.wait(te, "ve", bv)
        p.wait(s, "ve", bv)
        hi_cl()

    emit_B(p, scB, xB, preWo_ld, None, dwB, x1X, s0scr, combB)
    scB.close()

    hl = p.inc(sy.dma_start(out=haloX, in_=x1X[:, :, TOK - 128:TOK]), "st", 16)
    p.wait(g, "st", hl)
    p.inc(g.collective_compute(
        "AllGather", ALU.bypass, replica_groups=GROUPS,
        ins=[haloX.opt()], outs=[haloAG.opt()]), "cc", 1)

    scC = Scope(p, "C_")

    def x1_ld(x1sb):
        p.wait(sy, "cc", p.val("cc"))
        l1 = p.inc(sy.dma_start(out=x1sb[:, :, 128:EXT], in_=x1X), "w", 16)
        l2 = p.inc(sy.dma_start(
            out=x1sb[:, :, 0:128],
            in_=haloAG[0:128, :].rearrange("p (c f) -> p c f", c=8)), "w", 16)
        p.wait(v, "w", l2)
        v.tensor_scalar(x1sb[:, :, 0:128], x1sb[:, :, 0:128], hself_sb[:, 1:2],
                        None, op0=ALU.mult)
        bv = p.incd(v, "ve")
        for eng in (te, s):
            p.wait(eng, "ve", bv)
        p.wait(v, "w", l1)

    emit_C(p, scC, x1_ld, None, dwC, s1X, xn1X)
    scC.close()

    scD = Scope(p, "D_")
    emit_D(p, scD, xn1X, s1X, x1X, dwD, outD, combD)
    scD.close()
    return p.finish()


_progM = None


def kernel(**inputs):
    global _progM
    pr = prep_weights(inputs)
    x = np.ascontiguousarray(np.asarray(inputs["L0_attn_x"], np.float32))
    if _progM is None:
        _progM = build_mega()
    base = dict(
        WoT=pr["WoT"], gwT=pr["gwT0"], gb=pr["gb0"],
        w1g=pr["w1g"], w1l=pr["w1l"], w3g=pr["w3g"], w3l=pr["w3l"],
        sw1g=pr["sw1g"], sw1l=pr["sw1l"], sw3g=pr["sw3g"], sw3l=pr["sw3l"],
        w2=pr["w2"], sw2=pr["sw2"],
        b1g=pr["b1g"], b1l=pr["b1l"], b3g=pr["b3g"], b3l=pr["b3l"],
        sb1g=pr["sb1g"], sb1l=pr["sb1l"], sb3g=pr["sb3g"], sb3l=pr["sb3l"],
        b2=pr["b2"], sb2=pr["sb2"],
        wqT=pr["wqT"], wkT1=pr["wkT1"], wvT1=pr["wvT1"], woT1=pr["woT1"],
        gamma=pr["gamma"], beta=pr["beta"],
        gwT1=pr["gwT1"], egT=pr["egT"], euT=pr["euT"], edT=pr["edT"],
        sgT=pr["sgT"], suT=pr["suT"], sdT=pr["sdT"])
    in_maps = []
    for c in range(8):
        b, half = c // 2, c % 2
        im = dict(base)
        im["xA"] = np.ascontiguousarray(x[b])
        im["xB"] = np.ascontiguousarray(x[b, TOK * half:TOK * (half + 1)])
        im["wqkvA"] = np.ascontiguousarray(pr["wqkv"][:, :, 512 * half:512 * (half + 1)])
        hs = np.zeros((128, 2), BF)
        hs[:, 0] = 1.0 - half
        hs[:, 1] = half
        im["hsel"] = hs
        im["hself"] = hs.astype(np.float32)
        in_maps.append(im)
    res = run_bass_kernel_spmd(_progM, in_maps, core_ids=list(range(8)))
    out = np.empty((B, T, D), np.float32)
    for c in range(8):
        b, half = c // 2, c % 2
        out[b, TOK * half:TOK * (half + 1)] = np.asarray(res.results[c]["outD"])
    return out
